# version 03
# baseline (speedup 1.0000x reference)
import numpy as np
import ml_dtypes
import concourse.bass as bass
import concourse.mybir as mybir
from concourse.bass_core_rust import IndirectOffsetOnAxis
from concourse.bass_utils import run_bass_kernel_spmd

F32 = mybir.dt.float32
F16 = mybir.dt.float16
BF16 = mybir.dt.bfloat16
I32 = mybir.dt.int32
AF = mybir.ActivationFunctionType
ALU = mybir.AluOpType
AX = mybir.AxisListType.X

T = 1024
D = 4096
NC = 8
INV_SQRT_D = float(1.0 / np.sqrt(128.0))

ENGS = ("pe", "dv", "ac", "sp", "gp")
COMPUTE = ("pe", "dv", "ac")
STREAMS = ("st0", "st1", "st2", "st3")
ALLSEMS = ENGS + STREAMS
DMASEMS = ("sp", "gp") + STREAMS
DEBUG_SERIAL_P1 = False
SERIAL_P3 = False
SERIAL_P5A = False
SERIAL_P4 = False
SERIAL_P2 = False
SERIAL_P5B = False


class Sched:
    def __init__(self, nc):
        self.nc = nc
        self.q = {e: [] for e in ENGS}
        self.cnt = {s: 0 for s in ALLSEMS}
        self.wm = {e: {s: 0 for s in ALLSEMS} for e in ENGS}
        self.sems = {}

    def tok(self, sem):
        return (sem, self.cnt[sem])

    def emit(self, eng, fn, waits=(), inc=False, sem=None):
        real = []
        for (s, v) in waits:
            if v > 0 and self.wm[eng][s] < v:
                real.append((s, v))
                self.wm[eng][s] = v
        tgt = sem if sem is not None else eng
        n = 16 if eng in ("sp", "gp") else 1
        if inc:
            self.cnt[tgt] += n
        self.q[eng].append((tuple(real), fn, inc, n, tgt))
        return (tgt, self.cnt[tgt])

    def fence(self, eng):
        for s in DMASEMS:
            v = self.cnt[s]
            if v > 0 and self.wm[eng][s] < v:
                self.q[eng].append((((s, v),), None, False, 0, eng))
                self.wm[eng][s] = v

    def barrier(self):
        self.fence("sp")
        self.fence("gp")
        toks = [(s, self.cnt[s]) for s in ALLSEMS]
        for e in ENGS:
            for (s, v) in toks:
                if s == e and s in COMPUTE:
                    continue
                if v > 0 and self.wm[e][s] < v:
                    self.q[e].append((((s, v),), None, False, 0, e))
                    self.wm[e][s] = v

    def replay(self, block):
        def run(engobj, ename):
            for (waits, fn, inc, n, tgt) in self.q[ename]:
                for (s, v) in waits:
                    engobj.wait_ge(self.sems[s], v)
                if fn is not None:
                    ins = fn(engobj)
                    if inc:
                        ins.then_inc(self.sems[tgt], n)
        block.tensor(lambda e: run(e, "pe"))
        block.vector(lambda e: run(e, "dv"))
        block.scalar(lambda e: run(e, "ac"))
        block.sync(lambda e: run(e, "sp"))
        block.gpsimd(lambda e: run(e, "gp"))


def copy_on(engname):
    if engname == "ac":
        return lambda dst, src: (lambda e: e.copy(dst, src))
    return lambda dst, src: (lambda e: e.tensor_copy(dst, src))


def build_nc(debug=(), phases=6):
    nc = bass.Bass("TRN2", target_bir_lowering=False)

    x_in = nc.dram_tensor("x", [T, D], F32, kind="ExternalInput").ap()
    wq = nc.dram_tensor("wq", [D, 32, 128], F32, kind="ExternalInput").ap()
    wk = nc.dram_tensor("wk", [D, 8, 128], F32, kind="ExternalInput").ap()
    wv = nc.dram_tensor("wv", [D, 8, 128], F32, kind="ExternalInput").ap()
    bq = nc.dram_tensor("bq", [128, 32], F32, kind="ExternalInput").ap()
    bk = nc.dram_tensor("bk", [128, 8], F32, kind="ExternalInput").ap()
    bv = nc.dram_tensor("bv", [128, 8], F32, kind="ExternalInput").ap()
    qwt = nc.dram_tensor("qwt", [512, D], I32, kind="ExternalInput").ap()
    qzr = nc.dram_tensor("qzr", [32, 512], I32, kind="ExternalInput").ap()
    scl = nc.dram_tensor("scl", [32, D], F16, kind="ExternalInput").ap()
    gidx = nc.dram_tensor("gidx", [D], I32, kind="ExternalInput").ap()
    fbias = nc.dram_tensor("fbias", [D], F16, kind="ExternalInput").ap()
    identc = nc.dram_tensor("identc", [128, 128], BF16, kind="ExternalInput").ap()
    shiftc = nc.dram_tensor("shiftc", [128, 1], I32, kind="ExternalInput").ap()
    iotac = nc.dram_tensor("iotac", [128, 32], F32, kind="ExternalInput").ap()
    zshc = nc.dram_tensor("zshc", [1, D], I32, kind="ExternalInput").ap()

    out = nc.dram_tensor("out", [T, D], F32, kind="ExternalOutput").ap()

    od = nc.dram_tensor("od", [8, 128, T], BF16,
                        kind="ExternalOutput" if "od" in debug else "Internal").ap()
    oz = nc.dram_tensor("oz", [128, T], BF16,
                        kind="ExternalOutput" if "oz" in debug else "Internal").ap()
    zrd = nc.dram_tensor("zrd", [128, T], F32, kind="Internal").ap()
    dbg = {}
    for name, shape, dt in (("xn", [128, 32 * T], BF16),
                            ("qs", [128, 8 * T], BF16),
                            ("kk", [128, 8 * T], BF16),
                            ("vv", [128, 9 * T], BF16),
                            ("cs", [32, T], F16),
                            ("ffn", [128, 32 * T], F16),
                            ("acc", [128, 32 * T], BF16)):
        if name in debug:
            dbg[name] = nc.dram_tensor("dbg_" + name, shape, dt,
                                       kind="ExternalOutput").ap()

    S = Sched(nc)
    pe, dv, ac, sp, gp = ENGS

    live = []

    def alloc(name, shape, dt, where="sbuf", side=None):
        cm = (nc.sbuf_tensor(name, shape, dt, side=side) if where == "sbuf"
              else nc.psum_tensor(name, shape, dt))
        t = cm.__enter__()
        live.append((name, cm))
        return t

    def free(*names):
        for nm in names:
            for i, (n, cm) in enumerate(live):
                if n == nm:
                    cm.__exit__(None, None, None)
                    live.pop(i)
                    break

    PBIG = alloc("PBIG", [128, 2048], F32, "psum")
    pM = [PBIG[0:128, 0:512], PBIG[0:128, 512:1024]]
    pN = [PBIG[0:128, 1024:1536], PBIG[0:128, 1536:2048]]
    PO = [alloc("PO0", [128, 512], F32, "psum"), alloc("PO1", [128, 512], F32, "psum")]
    pX = [alloc(f"pX{i}", [128, 1024], BF16, "psum") for i in range(2)]

    ident = alloc("ident", [128, 128], BF16)
    sh8 = alloc("sh8", [128, 1], I32)
    gix = alloc("gix", [128, 32], I32)
    G = alloc("G", [128, 32 * 32], BF16)
    nzs = alloc("nzs", [32, D], F16)
    CS = alloc("CS", [32, T], F16)
    bqs = alloc("bqs", [128, 8], F32)
    bkt = alloc("bkt", [128, 8], F32)
    bvt = alloc("bvt", [128, 8], F32)
    XN = alloc("XN", [128, 32 * T], BF16, side="right")

    epsb = alloc("epsb", [128, 1], F32)
    iot = alloc("iot", [128, 32], F32)
    gixf = alloc("gixf", [128, 32], F32)
    qzrt = alloc("qzrt", [32, 512], I32)
    zsh = alloc("zsh", [32, D], I32)
    sclt = alloc("sclt", [32, D], F16)
    zint = alloc("zint", [32, D], I32)
    zf = alloc("zf", [32, D], F32)
    gtmp = alloc("gtmp", [128, 32], F32)
    bqraw = alloc("bqraw", [128, 32], F32)

    S.emit(sp, lambda e: e.dma_start(out=ident[:], in_=identc), inc=True)
    S.emit(sp, lambda e: e.dma_start(out=sh8[:], in_=shiftc), inc=True)
    for c in range(32):
        S.emit(sp, lambda e, c=c: e.dma_start(
            out=gix[:, c:c + 1], in_=gidx[c * 128:(c + 1) * 128].unsqueeze(1)),
            inc=True)
    S.emit(sp, lambda e: e.dma_start(out=iot[:], in_=iotac), inc=True)
    S.emit(sp, lambda e: e.dma_start(out=qzrt[:], in_=qzr), inc=True)
    S.emit(sp, lambda e: e.dma_start(
        out=zsh[:], in_=zshc.broadcast_to([32, D])), inc=True)
    S.emit(sp, lambda e: e.dma_start(out=sclt[:], in_=scl), inc=True)
    S.emit(sp, lambda e: e.dma_start(out=bqraw[:], in_=bq), inc=True)
    S.emit(sp, lambda e: e.dma_start(out=bkt[:], in_=bk), inc=True)
    S.emit(sp, lambda e: e.dma_start(out=bvt[:], in_=bv), inc=True)
    all0 = S.tok(sp)

    S.emit(dv, lambda e: e.memset(epsb[:], 1e-5), inc=True)

    gcv = S.emit(ac, lambda e: e.copy(gixf[:], gix[:]), waits=[all0], inc=True)
    g_free = None
    for c in range(32):
        ws = [all0, gcv] + ([g_free] if g_free else [])
        tk = S.emit(dv, lambda e, c=c: e.tensor_scalar(
            gtmp[:], iot[:], gixf[:, c:c + 1], None, op0=ALU.is_equal),
            waits=ws, inc=True)
        g_free = S.emit(ac, lambda e, c=c: e.copy(
            G[:, c * 32:(c + 1) * 32], gtmp[:]), waits=[tk], inc=True)
    t1 = S.emit(dv, lambda e: e.tensor_tensor(
        zint[:], qzrt[:].unsqueeze(2).broadcast_to([32, 512, 8]),
        zsh[:].rearrange("p (a b) -> p a b", b=8),
        op=ALU.logical_shift_right), waits=[all0], inc=True)
    t1 = S.emit(dv, lambda e: e.tensor_single_scalar(
        zint[:], zint[:], 15, op=ALU.bitwise_and), waits=[t1], inc=True)
    t2 = S.emit(ac, lambda e: e.copy(zf[:], zint[:]), waits=[t1], inc=True)
    t3 = S.emit(dv, lambda e: e.tensor_tensor(
        zf[:], zf[:], sclt[:], op=ALU.mult), waits=[t2], inc=True)
    S.emit(ac, lambda e: e.mul(nzs[:], zf[:], -1.0), waits=[t3], inc=True)
    t5 = S.emit(dv, lambda e: e.reduce_sum(
        bqs[:], bqraw[:].rearrange("p (a b) -> p a b", b=4), axis=AX),
        waits=[all0], inc=True)
    S.emit(ac, lambda e: e.mul(bqs[:], bqs[:], INV_SQRT_D), waits=[t5], inc=True)
    S.barrier()
    free("bqraw", "gtmp", "zf", "zint", "sclt", "zsh", "qzrt", "gixf", "iot")

    xtile = [alloc("xtile0", [128, D], F32), alloc("xtile1", [128, D], F32)]
    xnt = alloc("xnt", [128, D], BF16)
    sqs = alloc("sqs", [128, D], BF16)
    ssum = alloc("ssum", [128, 1], F32)
    ssq = alloc("ssq", [128, 1], F32)
    nmean = alloc("nmean", [128, 1], F32)
    varb = alloc("varb", [128, 1], F32)
    msq = alloc("msq", [128, 1], F32)
    rstd = alloc("rstd", [128, 1], F32)

    xload = []
    for i in range(2):
        xload.append(S.emit(sp, lambda e, i=i: e.dma_start(
            out=xtile[i][:], in_=x_in[i * 128:(i + 1) * 128, :]), inc=True,
            sem="st%d" % (i % 2)))
    last_evac = {}
    prev_tr = None
    for i in range(8):
        xt = xtile[i % 2]
        ldtok = xload[i]
        a = S.emit(dv, lambda e, xt=xt: e.reduce_sum(
            ssum[:], xt[:], axis=AX, negate=True), waits=[ldtok], inc=True)
        b = S.emit(ac, lambda e, xt=xt: e.activation(
            sqs[:], xt[:], AF.Square, accum_out=ssq[:]), waits=[ldtok], inc=True)
        c1 = S.emit(dv, lambda e: e.tensor_single_scalar(
            nmean[:], ssum[:], 1.0 / D, op=ALU.mult), waits=[a], inc=True)
        c2 = S.emit(dv, lambda e: e.tensor_tensor(
            msq[:], nmean[:], nmean[:], op=ALU.mult), waits=[c1], inc=True)
        c3 = S.emit(dv, lambda e: e.scalar_tensor_tensor(
            varb[:], ssq[:], 1.0 / D, msq[:], op0=ALU.mult, op1=ALU.subtract),
            waits=[b, c2], inc=True)
        c4 = S.emit(ac, lambda e: e.activation(
            varb[:], varb[:], AF.Ln, bias=epsb[:, 0:1]), waits=[c3], inc=True)
        c5 = S.emit(ac, lambda e: e.activation(
            rstd[:], varb[:], AF.Exp, scale=-0.5), waits=[c4], inc=True)
        ws6 = [c5] + ([prev_tr] if prev_tr else [])
        c6 = S.emit(dv, lambda e, xt=xt: e.tensor_scalar(
            xnt[:], xt[:], nmean[:, 0:1], rstd[:, 0:1],
            op0=ALU.add, op1=ALU.mult), waits=ws6, inc=True)
        if i + 2 < 8:
            xload.append(S.emit(sp, lambda e, i=i: e.dma_start(
                out=xtile[i % 2][:], in_=x_in[(i + 2) * 128:(i + 3) * 128, :]),
                waits=[c6], inc=True, sem="st%d" % (i % 2)))
        for c in range(32):
            bk_i = (i * 32 + c) % 2
            ws = [c6] + ([last_evac[bk_i]] if bk_i in last_evac else [])
            tt = S.emit(pe, lambda e, c=c, bk_i=bk_i: e.transpose(
                pX[bk_i][:, 0:128], xnt[:, c * 128:(c + 1) * 128], ident[:]),
                waits=ws, inc=True)
            eng = dv if (c % 2) else ac
            mk = copy_on(eng)
            ev = S.emit(eng, mk(
                XN[:, c * T + i * 128: c * T + (i + 1) * 128], pX[bk_i][:, 0:128]),
                waits=[tt], inc=True)
            last_evac[bk_i] = ev
        prev_tr = S.tok(pe)
        if DEBUG_SERIAL_P1:
            S.barrier()
    S.barrier()
    free("rstd", "msq", "varb", "nmean", "ssq", "ssum", "sqs", "xnt",
         "xtile1", "xtile0")
    if "xn" in dbg:
        S.emit(sp, lambda e: e.dma_start(out=dbg["xn"], in_=XN[:]), inc=True)
        S.barrier()
    if phases <= 1:
        return finish(nc, S, live)

    QSI = alloc("QSI", [128, 8 * T], BF16)
    KI = alloc("KI", [128, 8 * T], BF16)
    VI = alloc("VI", [128, 9 * T], BF16)
    WG = alloc("WG", [128, 2 * 32 * 256], BF16)
    wacc = [alloc(f"wacc{i}", [128, 256], F32) for i in range(4)]

    S.emit(dv, lambda e: e.memset(VI[:], 1.0), inc=True)
    vi_ones = S.tok(dv)

    groups = ([("qs", g) for g in range(4)] + [("k", g) for g in range(4)]
              + [("v", g) for g in range(4)])
    wq_v = wq.rearrange("e (hk j) d -> e hk j d", j=4)
    group_done = {}
    wg_ready = [None] * 32
    chain_tok = [None] * 4

    for gi, (kind, g) in enumerate(groups):
        wgb = (gi % 2) * 32 * 256
        if kind != "qs":
            S.fence(gp)
        for ei in range(32):
            w_wait = []
            if gi >= 2:
                w_wait.append(group_done[gi - 2])
            if kind == "qs":
                cb = ei % 4
                sname = "st%d" % cb
                if chain_tok[cb] is not None:
                    w_wait.append(chain_tok[cb])
                prev = None
                for j in range(4):
                    src = wq_v[ei * 128:(ei + 1) * 128, 2 * g:2 * g + 2, j, :]
                    acc_op = ALU.bypass if j == 0 else ALU.add
                    ww = list(w_wait) if j == 0 else [prev]
                    prev = S.emit(gp, lambda e, src=src, cb=cb, acc_op=acc_op:
                                  e.dma_start(out=wacc[cb][:], in_=src,
                                              accum_op=acc_op),
                                  waits=ww, inc=True, sem=sname)
                cv = S.emit(ac, lambda e, ei=ei, cb=cb, wgb=wgb: e.activation(
                    WG[:, wgb + ei * 256:wgb + (ei + 1) * 256], wacc[cb][:],
                    AF.Copy, scale=INV_SQRT_D), waits=[prev] + w_wait, inc=True)
                chain_tok[cb] = cv
                wg_ready[ei] = cv
            else:
                wsrc = wk if kind == "k" else wv
                src = wsrc[ei * 128:(ei + 1) * 128, 2 * g:2 * g + 2, :]
                wg_ready[ei] = S.emit(gp, lambda e, src=src, ei=ei, wgb=wgb:
                    e.dma_start(out=WG[:, wgb + ei * 256:wgb + (ei + 1) * 256],
                                in_=src), waits=w_wait, inc=True)
        banks = [pM[0], pM[1], pN[0], pN[1]]
        grp_load_tok = wg_ready[31]
        for ei in range(32):
            for mi in range(2):
                for hf in range(2):
                    bkk = banks[mi * 2 + hf]
                    ws = []
                    if mi == 0 and hf == 0:
                        ws.append(wg_ready[ei] if kind == "qs" else grp_load_tok)
                        if ei == 0 and gi >= 1:
                            ws.append(group_done[gi - 1])
                    S.emit(pe, lambda e, ei=ei, mi=mi, hf=hf, bkk=bkk, wgb=wgb:
                        e.matmul(
                        bkk,
                        WG[:, wgb + ei * 256 + mi * 128:
                           wgb + ei * 256 + (mi + 1) * 128],
                        XN[:, ei * T + hf * 512: ei * T + hf * 512 + 512],
                        start=(ei == 0), stop=(ei == 31)),
                        waits=ws, inc=(ei == 31 and mi == 1 and hf == 1))
        mm_tok = S.tok(pe)
        for mi in range(2):
            for hf in range(2):
                bkk = banks[mi * 2 + hf]
                hk = 2 * g + mi
                if kind == "qs":
                    dst = QSI[:].rearrange("p (t j) -> p t j", j=8)[
                        :, hf * 512:(hf + 1) * 512, hk]
                    bias_col = bqs[:, hk:hk + 1]
                elif kind == "k":
                    dst = KI[:].rearrange("p (t j) -> p t j", j=8)[
                        :, hf * 512:(hf + 1) * 512, hk]
                    bias_col = bkt[:, hk:hk + 1]
                else:
                    dst = VI[:].rearrange("p (t j) -> p t j", j=9)[
                        :, hf * 512:(hf + 1) * 512, hk]
                    bias_col = bvt[:, hk:hk + 1]
                w = [mm_tok] + ([vi_ones] if kind == "v" else [])
                S.emit(dv, lambda e, dst=dst, bkk=bkk, bias_col=bias_col:
                       e.tensor_scalar(dst, bkk, bias_col, None, op0=ALU.add),
                       waits=w, inc=True)
        group_done[gi] = S.tok(dv)
        if SERIAL_P2:
            S.barrier()
    S.barrier()
    free("wacc3", "wacc2", "wacc1", "wacc0", "WG")
    for nm, buf in (("qs", QSI), ("kk", KI), ("vv", VI)):
        if nm in dbg:
            S.emit(sp, lambda e, nm=nm, buf=buf: e.dma_start(
                out=dbg[nm], in_=buf[:]), inc=True)
    if any(n in dbg for n in ("qs", "kk", "vv")):
        S.barrier()
    if phases <= 2:
        return finish(nc, S, live)

    QSw = [alloc(f"QSw{i}", [128, 128], BF16) for i in range(2)]
    Kw = [alloc(f"Kw{i}", [128, 128], BF16) for i in range(2)]
    QTa = [alloc(f"QTa{i}", [128, 128], BF16) for i in range(2)]
    QTb = [alloc(f"QTb{i}", [128, 128], BF16) for i in range(2)]
    KT = [alloc(f"KT{i}", [128, 128], BF16) for i in range(2)]
    A4 = [alloc(f"A4{i}", [128, 512], BF16) for i in range(2)]
    m4 = [alloc(f"m4{i}", [128, 4], F32) for i in range(2)]
    OT = alloc("OT", [9, 128 * 128], BF16)

    for i in range(2):
        S.emit(dv, lambda e, i=i: e.memset(Kw[i][:], 1.0), inc=True)
    kinit = S.tok(dv)

    import os as _os
    NW = int(_os.environ.get("NW_LIMIT", str(T // 4)))
    tok_build = [None] * NW
    tok_mcopy = [None] * NW
    tok_evA = [None] * NW
    tok_evK = [None] * NW
    tok_evB = [None] * NW
    tok_S = [None] * NW
    tok_ST = [None] * NW
    tok_max = [None] * NW
    tok_exp = [None] * NW
    tok_out = [None] * NW
    tok_oev = [None] * NW
    tok_dump = {}
    trb = [0]

    def emit_build(w):
        b = w % 2
        ws = [kinit]
        if w >= 2:
            ws += [tok_evB[w - 2], tok_evK[w - 2]]
        t0 = w * 4
        s_q = QSI[:, 8 * t0: 8 * t0 + 32].rearrange("p (a c) -> p a c", c=8)
        d_q = QSw[b][:].rearrange("p (a c) -> p a c", c=32)[:, :, 0:8]
        S.emit(dv, lambda e, s_q=s_q, d_q=d_q: e.tensor_copy(d_q, s_q),
               waits=ws, inc=True)
        s_k = KI[:, 8 * t0: 8 * t0 + 32].rearrange("p (a c) -> p a c", c=8)
        d_k = Kw[b][:].rearrange("p (a c) -> p a c", c=32)[:, :, 0:8]
        tok_build[w] = S.emit(dv, lambda e, s_k=s_k, d_k=d_k: e.tensor_copy(
            d_k, s_k), inc=True)

    def tr_into(src_tile, dst_tile, wait_toks):
        i0 = trb[0] % 2
        trb[0] += 1
        ws = list(wait_toks)
        if i0 in last_evac:
            ws.append(last_evac[i0])
        tt = S.emit(pe, lambda e, i0=i0, src_tile=src_tile: e.transpose(
            pX[i0][:, 0:128], src_tile, ident[:]), waits=ws, inc=True)
        ev = S.emit(ac, lambda e, i0=i0, dst_tile=dst_tile: e.copy(
            dst_tile, pX[i0][:, 0:128]), waits=[tt], inc=True)
        last_evac[i0] = ev
        return ev

    def emit_tA_tK(w):
        b = w % 2
        tok_evA[w] = tr_into(QSw[b][:], QTa[b][:], [tok_build[w]])
        tok_evK[w] = tr_into(Kw[b][:], KT[b][:], [tok_build[w]])

    def emit_S(w):
        b = w % 2
        ws = [tok_evA[w], tok_evK[w]]
        for tl in range(4):
            S.emit(pe, lambda e, b=b, tl=tl: e.matmul(
                PBIG[0:128, 512 * tl: 512 * tl + 128],
                QTa[b][32 * tl:32 * tl + 8, :],
                KT[b][32 * tl:32 * tl + 8, :],
                start=True, stop=True, tile_position=(32 * tl, 0)),
                waits=ws if tl == 0 else [], inc=(tl == 3))
        tok_S[w] = S.tok(pe)

    def emit_max(w):
        b = w % 2
        sview = PBIG[:].rearrange("p (a c) -> p a c", a=4)[:, :, 0:128]
        tok_max[w] = S.emit(dv, lambda e, b=b, sview=sview: e.reduce_max(
            m4[b][:], sview, axis=AX,
            negate=True), waits=[tok_S[w]], inc=True)
        d_m = QSw[b][:].rearrange("p (a c) -> p a c", c=32)[:, :, 8]
        tok_mcopy[w] = S.emit(dv, lambda e, b=b, d_m=d_m: e.tensor_copy(
            d_m.unsqueeze(2), m4[b][:].unsqueeze(2)), waits=[tok_max[w]],
            inc=True)

    def emit_tB(w):
        b = w % 2
        tok_evB[w] = tr_into(QSw[b][:], QTb[b][:], [tok_mcopy[w]])

    def emit_ST(w):
        b = w % 2
        ws = [tok_evB[w]]
        for tl in range(4):
            S.emit(pe, lambda e, b=b, tl=tl: e.matmul(
                PBIG[0:128, 512 * tl + 128: 512 * tl + 256],
                KT[b][32 * tl:32 * tl + 9, :],
                QTb[b][32 * tl:32 * tl + 9, :],
                start=True, stop=True, tile_position=(32 * tl, 0)),
                waits=ws if tl == 0 else [], inc=(tl == 3))
        tok_ST[w] = S.tok(pe)

    def emit_exp(w):
        b = w % 2
        stview = PBIG[:].rearrange("p (a c) -> p a c", a=4)[:, :, 128:256]
        dstv = A4[b][:].rearrange("p (a c) -> p a c", a=4)
        tok_exp[w] = S.emit(ac, lambda e, b=b, stview=stview, dstv=dstv:
            e.activation(dstv, stview, AF.Exp), waits=[tok_ST[w]], inc=True)

    def emit_out(w):
        b = w % 2
        po = PO[w % 2]
        ws = [tok_exp[w], tok_max[w]]
        if w >= 2:
            ws.append(tok_oev[w - 2])
        for tl in range(4):
            t = w * 4 + tl
            S.emit(pe, lambda e, po=po, b=b, tl=tl, t=t: e.matmul(
                po[0:9, tl * 128:(tl + 1) * 128],
                VI[:, 9 * t: 9 * t + 9],
                A4[b][:, tl * 128:(tl + 1) * 128],
                start=True, stop=True),
                waits=ws if tl == 0 else [], inc=(tl == 3))
        tok_out[w] = S.tok(pe)

    def emit_oev(w):
        po = PO[w % 2]
        wl = w % 32
        seg = w // 32
        ws = [tok_out[w]]
        if (seg - 1) in tok_dump:
            ws.append(tok_dump[seg - 1])
        src = po[0:9, :].rearrange("p (t d) -> p d t", t=4)
        dst = OT[:].rearrange("p (d t) -> p d t", t=128)[:, :, 4 * wl:4 * wl + 4]
        tok_oev[w] = S.emit(dv, lambda e, src=src, dst=dst: e.tensor_copy(
            dst, src), waits=ws, inc=True)

    def emit_dump(seg):
        ws = [tok_oev[seg * 32 + 31]]
        src8 = OT[0:8, :].rearrange("p (d t) -> p d t", t=128)
        S.emit(sp, lambda e, src8=src8, seg=seg: e.dma_start(
            out=od[:, :, seg * 128:(seg + 1) * 128], in_=src8),
            waits=ws, inc=True)
        srcz = OT[8:9, :].rearrange("p (d t) -> p d t", t=128)
        tok_dump[seg] = S.emit(sp, lambda e, srcz=srcz, seg=seg: e.dma_start(
            out=oz[:, seg * 128:(seg + 1) * 128].unsqueeze(0), in_=srcz), inc=True)

    P3S = int(_os.environ.get("P3_STEPS", "8"))
    emit_build(0)
    emit_tA_tK(0)
    for w in range(NW):
        if P3S >= 2:
            emit_S(w)
        if w + 1 < NW:
            emit_build(w + 1)
            emit_tA_tK(w + 1)
        if P3S >= 3:
            emit_max(w)
        if P3S >= 4:
            emit_tB(w)
        if P3S >= 5:
            emit_ST(w)
        if P3S >= 6:
            emit_exp(w)
        if P3S >= 7:
            emit_out(w)
        if P3S >= 8:
            emit_oev(w)
        if P3S >= 8 and w % 32 == 31:
            emit_dump(w // 32)
        if SERIAL_P3:
            S.barrier()
    S.barrier()
    free("OT", "m41", "m40", "A41", "A40", "KT1", "KT0", "QTb1", "QTb0",
         "QTa1", "QTa0", "Kw1", "Kw0", "QSw1", "QSw0", "VI", "KI", "QSI")

    if phases <= 3:
        return finish(nc, S, live)

    FFN = alloc("FFN", [128, 32 * T], F16)
    qwr = [alloc(f"qwr{i}", [128, 256], I32) for i in range(2)]
    srep = [alloc(f"srep{i}", [128, 256], BF16) for i in range(2)]
    nibi = [alloc(f"nibi{i}", [128, 256], I32) for i in range(2)]
    nibb = [alloc(f"nibb{i}", [128, 256], BF16) for i in range(2)]
    Wt = [alloc(f"Wt{i}", [128, 256], BF16) for i in range(2)]

    cs_half = []
    for hf in range(2):
        ws0 = [cs_half[-1]] if cs_half else []
        for k in range(32):
            S.emit(pe, lambda e, hf=hf, k=k: e.matmul(
                PBIG[0:32, 0:512], G[:, k * 32:(k + 1) * 32],
                XN[:, k * T + hf * 512: k * T + hf * 512 + 512],
                start=(k == 0), stop=(k == 31)),
                waits=ws0 if k == 0 else [], inc=(k == 31))
        cst = S.tok(pe)
        cs_half.append(S.emit(dv, lambda e, hf=hf: e.tensor_copy(
            CS[:, hf * 512:(hf + 1) * 512], PBIG[0:32, 0:512]),
            waits=[cst], inc=True))
    cs_done = cs_half[-1]

    banksF = [pM[0], pM[1], pN[0], pN[1]]
    deq_free = [None, None]
    evac_done = None
    for jg in range(16):
        n0 = jg * 256
        for k in range(32):
            par = k % 2
            ws = [deq_free[par]] if deq_free[par] is not None else []
            ld1 = S.emit(gp, lambda e, k=k, n0=n0, par=par: e.dma_start(
                out=qwr[par][:],
                in_=qwt[16 * k:16 * k + 16, n0:n0 + 256].unsqueeze(1)
                .broadcast_to([16, 8, 256])), waits=ws, inc=True,
                sem="st%d" % par)
            ld2 = S.emit(gp, lambda e, k=k, n0=n0, par=par: e.indirect_dma_start(
                out=srep[par][:], out_offset=None,
                in_=scl,
                in_offset=IndirectOffsetOnAxis(gix[:, k:k + 1], 0),
                element_offset=n0), inc=True, sem="st%d" % (2 + par))
            d1 = S.emit(dv, lambda e, par=par: e.tensor_scalar(
                nibi[par][:], qwr[par][:], sh8[:, 0:1], 15,
                op0=ALU.logical_shift_right, op1=ALU.bitwise_and),
                waits=[ld1] + ws, inc=True)
            d2 = S.emit(ac, lambda e, par=par: e.copy(
                nibb[par][:], nibi[par][:]), waits=[d1], inc=True)
            d3 = S.emit(dv, lambda e, par=par: e.tensor_tensor(
                Wt[par][:], nibb[par][:], srep[par][:], op=ALU.mult),
                waits=[d2, ld2], inc=True)
            ws2 = [d3]
            if k == 0:
                ws2.append(cs_done if jg == 0 else evac_done)
            for jl in range(2):
                for hf in range(2):
                    bkk = banksF[jl * 2 + hf]
                    S.emit(pe, lambda e, par=par, jl=jl, hf=hf, bkk=bkk, k=k:
                           e.matmul(bkk,
                                    Wt[par][:, jl * 128:(jl + 1) * 128],
                                    XN[:, k * T + hf * 512: k * T + hf * 512 + 512],
                                    start=(k == 0), stop=False),
                           waits=ws2 if (jl == 0 and hf == 0) else [],
                           inc=(jl == 1 and hf == 1))
            deq_free[par] = S.tok(pe)
        for jl in range(2):
            for hf in range(2):
                bkk = banksF[jl * 2 + hf]
                j = 2 * jg + jl
                S.emit(pe, lambda e, j=j, hf=hf, bkk=bkk: e.matmul(
                    bkk, nzs[:, j * 128:(j + 1) * 128],
                    CS[:, hf * 512:(hf + 1) * 512],
                    start=False, stop=True),
                    inc=(jl == 1 and hf == 1))
        fmt = S.tok(pe)
        for jl in range(2):
            for hf in range(2):
                bkk = banksF[jl * 2 + hf]
                j = 2 * jg + jl
                S.emit(ac, lambda e, j=j, hf=hf, bkk=bkk: e.copy(
                    FFN[:, j * T + hf * 512: j * T + hf * 512 + 512], bkk),
                    waits=[fmt], inc=True)
        evac_done = S.tok(ac)
        if SERIAL_P4:
            S.barrier()
    S.barrier()
    free("Wt1", "Wt0", "nibb1", "nibb0", "nibi1", "nibi0", "srep1", "srep0",
         "qwr1", "qwr0", "XN")
    if "cs" in dbg:
        S.emit(sp, lambda e: e.dma_start(out=dbg["cs"], in_=CS[:]), inc=True)
    if "ffn" in dbg:
        S.emit(sp, lambda e: e.dma_start(out=dbg["ffn"], in_=FFN[:]), inc=True)
    if any(n in dbg for n in ("cs", "ffn")):
        S.barrier()
    if phases <= 4:
        return finish(nc, S, live)

    ACC = alloc("ACC", [128, 32 * T], BF16, side="right")
    zfb = alloc("zfb", [128, T], F32)
    zfr = alloc("zfr", [128, T], F32)
    attf = [alloc(f"attf{i}", [128, T], F32) for i in range(2)]
    zrep = [alloc(f"zrep{i}", [128, T], F32) for i in range(2)]
    tmpc = [alloc(f"tmpc{i}", [128, T], F32) for i in range(2)]

    zl = S.emit(gp, lambda e: e.dma_start(out=zfb[:], in_=oz), inc=True)
    zr_ = S.emit(dv, lambda e: e.reciprocal(zfr[:], zfb[:]), waits=[zl], inc=True)
    zd = S.emit(sp, lambda e: e.dma_start(out=zrd, in_=zfr[:]), waits=[zr_],
                inc=True)

    comb_tok = [None, None]
    for F_ in range(32):
        par = F_ % 2
        ws = [zd] + ([comb_tok[par]] if comb_tok[par] is not None else [])
        rds = []
        for dl in range(4):
            src = od[:, 4 * F_ + dl, :].unsqueeze(1).broadcast_to([8, 4, T])
            rds.append(S.emit(gp, lambda e, src=src, dl=dl, par=par: e.dma_start(
                out=attf[par][32 * dl:32 * dl + 32, :], in_=src),
                waits=ws if dl == 0 else [], inc=True, sem="st%d" % par))
        zsrc = zrd[4 * F_:4 * F_ + 4, :].unsqueeze(1).broadcast_to([4, 32, T])
        rz = S.emit(sp, lambda e, zsrc=zsrc, par=par: e.dma_start(
            out=zrep[par][:], in_=zsrc), waits=ws, inc=True,
            sem="st%d" % (2 + par))
        m1 = S.emit(dv, lambda e, par=par: e.tensor_tensor(
            tmpc[par][:], attf[par][:], zrep[par][:], op=ALU.mult),
            waits=rds + [rz], inc=True)
        comb_tok[par] = S.emit(dv, lambda e, par=par, F_=F_: e.tensor_tensor(
            ACC[:, F_ * T:(F_ + 1) * T], tmpc[par][:],
            FFN[:, F_ * T:(F_ + 1) * T], op=ALU.add), waits=[m1], inc=True)
        if SERIAL_P5A:
            S.barrier()
    S.barrier()
    free("tmpc1", "tmpc0", "zrep1", "zrep0", "attf1", "attf0", "zfr", "zfb",
         "FFN")
    if "acc" in dbg:
        S.emit(sp, lambda e: e.dma_start(out=dbg["acc"], in_=ACC[:]), inc=True)
        S.barrier()
    if phases <= 5:
        return finish(nc, S, live)

    xt2 = [alloc("xt2_0", [128, D], F32), alloc("xt2_1", [128, D], F32)]
    xpb = alloc("xpb", [128, D], F32)
    otok = alloc("otok", [128, D], F32)
    brep = alloc("brep", [128, D], F16)

    bl = S.emit(sp, lambda e: e.dma_start(
        out=brep[:], in_=fbias.unsqueeze(0).broadcast_to([128, D])), inc=True,
        sem="st2")
    x2load = []
    for i in range(2):
        x2load.append(S.emit(sp, lambda e, i=i: e.dma_start(
            out=xt2[i][:], in_=x_in[i * 128:(i + 1) * 128, :]), inc=True,
            sem="st%d" % (i % 2)))
    le5 = {}
    st_tok = None
    prev_evs = None
    for i in range(8):
        xb_ = xt2[i % 2]
        wsx = [x2load[i], bl] + ([prev_evs] if prev_evs else [])
        xp = S.emit(dv, lambda e, xb_=xb_: e.tensor_tensor(
            xpb[:], xb_[:], brep[:], op=ALU.add), waits=wsx, inc=True)
        if i + 2 < 8:
            x2load.append(S.emit(sp, lambda e, i=i: e.dma_start(
                out=xt2[i % 2][:], in_=x_in[(i + 2) * 128:(i + 3) * 128, :]),
                waits=[xp], inc=True, sem="st%d" % (i % 2)))
        evs = []
        for F_ in range(32):
            bk_i = (i * 32 + F_) % 2
            ws = [xp]
            if st_tok is not None:
                ws.append(st_tok)
            if bk_i in le5:
                ws.append(le5[bk_i])
            tt = S.emit(pe, lambda e, F_=F_, i=i, bk_i=bk_i: e.transpose(
                pX[bk_i][:, 0:128],
                ACC[:, F_ * T + i * 128: F_ * T + (i + 1) * 128], ident[:]),
                waits=ws, inc=True)
            ev = S.emit(dv, lambda e, F_=F_, bk_i=bk_i: e.tensor_tensor(
                otok[:, F_ * 128:(F_ + 1) * 128], pX[bk_i][:, 0:128],
                xpb[:, F_ * 128:(F_ + 1) * 128], op=ALU.add),
                waits=[tt, xp], inc=True)
            le5[bk_i] = ev
            evs.append(ev)
        st_tok = S.emit(sp, lambda e, i=i: e.dma_start(
            out=out[i * 128:(i + 1) * 128, :], in_=otok[:]),
            waits=evs, inc=True, sem="st3")
        if SERIAL_P5B:
            S.barrier()
        ac_last = max((v for (s, v) in evs if s == "ac"), default=0)
        prev_evs = ("ac", ac_last)
    S.emit(sp, lambda e: e.nop(), waits=[st_tok])
    S.barrier()
    return finish(nc, S, live)


def finish(nc, S, live):
    from contextlib import ExitStack
    with ExitStack() as ctx:
        for e in ALLSEMS:
            S.sems[e] = ctx.enter_context(nc.semaphore("sem_" + e))
        with nc.Block() as blk0:
            @blk0.gpsimd
            def _(g):
                for snm in ALLSEMS:
                    g.sem_clear(S.sems[snm])
        block = ctx.enter_context(nc.Block())
        S.replay(block)
    for n, cm in reversed(live):
        cm.__exit__(None, None, None)
    return nc


def make_inputs(inp, core):
    bf = ml_dtypes.bfloat16
    x = np.ascontiguousarray(np.asarray(inp["L0_attn_x"]).reshape(8192, 4096)[
        core * T:(core + 1) * T])
    return dict(
        x=x,
        wq=np.ascontiguousarray(np.transpose(np.asarray(inp["L0_attn_Wq"]), (0, 2, 1))),
        wk=np.ascontiguousarray(np.transpose(np.asarray(inp["L0_attn_Wk"]), (0, 2, 1))),
        wv=np.ascontiguousarray(np.transpose(np.asarray(inp["L0_attn_Wv"]), (0, 2, 1))),
        bq=np.ascontiguousarray(np.asarray(inp["L0_attn_bq"])),
        bk=np.ascontiguousarray(np.asarray(inp["L0_attn_bk"])),
        bv=np.ascontiguousarray(np.asarray(inp["L0_attn_bv"])),
        qwt=np.ascontiguousarray(np.asarray(inp["L0_ffn_qweight"])),
        qzr=np.ascontiguousarray(np.asarray(inp["L0_ffn_qzeros"])),
        scl=np.ascontiguousarray(np.asarray(inp["L0_ffn_scales"])),
        gidx=np.ascontiguousarray(np.asarray(inp["L0_ffn_g_idx"])),
        fbias=np.ascontiguousarray(np.asarray(inp["L0_ffn_bias"])),
        identc=np.eye(128, dtype=bf),
        shiftc=(4 * (np.arange(128, dtype=np.int32) % 8)).reshape(128, 1),
        iotac=np.broadcast_to(np.arange(32, dtype=np.float32), (128, 32)).copy(),
        zshc=(4 * (np.arange(4096, dtype=np.int32) % 8)).reshape(1, 4096),
    )


def kernel(**inputs):
    nc = build_nc()
    in_maps = [make_inputs(inputs, c) for c in range(NC)]
    res = run_bass_kernel_spmd(nc, in_maps, core_ids=list(range(NC)))
    outs = [np.asarray(res.results[c]["out"]) for c in range(NC)]
    return np.concatenate(outs, axis=0).reshape(4, 2048, 4096).astype(np.float32)


# version 05 (final)
# speedup vs baseline: 1.9441x; 1.9441x over previous
import numpy as np
import ml_dtypes
import concourse.bass as bass
import concourse.mybir as mybir
from concourse.bass_core_rust import IndirectOffsetOnAxis
from concourse.bass_utils import run_bass_kernel_spmd

F32 = mybir.dt.float32
F16 = mybir.dt.float16
BF16 = mybir.dt.bfloat16
I32 = mybir.dt.int32
AF = mybir.ActivationFunctionType
ALU = mybir.AluOpType
AX = mybir.AxisListType.X

T = 1024
D = 4096
NC = 8
INV_SQRT_D = float(1.0 / np.sqrt(128.0))

ENGS = ("pe", "dv", "ac", "sp", "gp")
COMPUTE = ("pe", "dv", "ac")
STREAMS = tuple("st%d" % i for i in range(12))
ALLSEMS = ENGS + STREAMS
DMASEMS = ("sp", "gp") + STREAMS
DEBUG_SERIAL_P1 = False
SERIAL_P3 = False
SERIAL_P5A = False
SERIAL_P4 = False
SERIAL_P2 = False
SERIAL_P5B = False


class Sched:
    def __init__(self, nc):
        self.nc = nc
        self.q = {e: [] for e in ENGS}
        self.cnt = {s: 0 for s in ALLSEMS}
        self.wm = {e: {s: 0 for s in ALLSEMS} for e in ENGS}
        self.sems = {}

    def tok(self, sem):
        return (sem, self.cnt[sem])

    def emit(self, eng, fn, waits=(), inc=False, sem=None):
        real = []
        for (s, v) in waits:
            if v > 0 and self.wm[eng][s] < v:
                real.append((s, v))
                self.wm[eng][s] = v
        tgt = sem if sem is not None else eng
        n = 16 if eng in ("sp", "gp") else 1
        if inc:
            self.cnt[tgt] += n
        self.q[eng].append((tuple(real), fn, inc, n, tgt))
        return (tgt, self.cnt[tgt])

    def fence(self, eng):
        for s in DMASEMS:
            v = self.cnt[s]
            if v > 0 and self.wm[eng][s] < v:
                self.q[eng].append((((s, v),), None, False, 0, eng))
                self.wm[eng][s] = v

    def barrier(self):
        self.fence("sp")
        self.fence("gp")
        toks = [(s, self.cnt[s]) for s in ALLSEMS]
        for e in ENGS:
            for (s, v) in toks:
                if s == e and s in COMPUTE:
                    continue
                if v > 0 and self.wm[e][s] < v:
                    self.q[e].append((((s, v),), None, False, 0, e))
                    self.wm[e][s] = v

    def replay(self, block):
        def run(engobj, ename):
            for (waits, fn, inc, n, tgt) in self.q[ename]:
                for (s, v) in waits:
                    engobj.wait_ge(self.sems[s], v)
                if fn is not None:
                    ins = fn(engobj)
                    if inc:
                        ins.then_inc(self.sems[tgt], n)
        block.tensor(lambda e: run(e, "pe"))
        block.vector(lambda e: run(e, "dv"))
        block.scalar(lambda e: run(e, "ac"))
        block.sync(lambda e: run(e, "sp"))
        block.gpsimd(lambda e: run(e, "gp"))


def copy_on(engname):
    if engname == "ac":
        return lambda dst, src: (lambda e: e.copy(dst, src))
    return lambda dst, src: (lambda e: e.tensor_copy(dst, src))


def build_nc(debug=(), phases=6):
    nc = bass.Bass("TRN2", target_bir_lowering=False)

    x_in = nc.dram_tensor("x", [T, D], F32, kind="ExternalInput").ap()
    wq = nc.dram_tensor("wq", [D, 32, 128], F32, kind="ExternalInput").ap()
    wk = nc.dram_tensor("wk", [D, 8, 128], F32, kind="ExternalInput").ap()
    wv = nc.dram_tensor("wv", [D, 8, 128], F32, kind="ExternalInput").ap()
    bq = nc.dram_tensor("bq", [128, 32], F32, kind="ExternalInput").ap()
    bk = nc.dram_tensor("bk", [128, 8], F32, kind="ExternalInput").ap()
    bv = nc.dram_tensor("bv", [128, 8], F32, kind="ExternalInput").ap()
    qwt = nc.dram_tensor("qwt", [512, D], I32, kind="ExternalInput").ap()
    qzr = nc.dram_tensor("qzr", [32, 512], I32, kind="ExternalInput").ap()
    scl = nc.dram_tensor("scl", [32, D], F16, kind="ExternalInput").ap()
    gidx = nc.dram_tensor("gidx", [D], I32, kind="ExternalInput").ap()
    fbias = nc.dram_tensor("fbias", [D], F16, kind="ExternalInput").ap()
    identc = nc.dram_tensor("identc", [128, 128], BF16, kind="ExternalInput").ap()
    shiftc = nc.dram_tensor("shiftc", [128, 1], I32, kind="ExternalInput").ap()
    iotac = nc.dram_tensor("iotac", [128, 32], F32, kind="ExternalInput").ap()
    zshc = nc.dram_tensor("zshc", [1, D], I32, kind="ExternalInput").ap()

    out = nc.dram_tensor("out", [T, D], F32, kind="ExternalOutput").ap()

    od = nc.dram_tensor("od", [8, 128, T], BF16,
                        kind="ExternalOutput" if "od" in debug else "Internal").ap()
    oz = nc.dram_tensor("oz", [128, T], BF16,
                        kind="ExternalOutput" if "oz" in debug else "Internal").ap()
    zrd = nc.dram_tensor("zrd", [128, T], F32, kind="Internal").ap()
    srd = nc.dram_tensor("srd", [32, 128, D], BF16, kind="Internal").ap()
    dbg = {}
    for name, shape, dt in (("xn", [128, 32 * T], BF16),
                            ("qs", [128, 8 * T], BF16),
                            ("kk", [128, 8 * T], BF16),
                            ("vv", [128, 9 * T], BF16),
                            ("cs", [32, T], F16),
                            ("acc", [128, 32 * T], BF16)):
        if name in debug:
            dbg[name] = nc.dram_tensor("dbg_" + name, shape, dt,
                                       kind="ExternalOutput").ap()

    S = Sched(nc)
    pe, dv, ac, sp, gp = ENGS

    live = []

    def alloc(name, shape, dt, where="sbuf", side=None):
        cm = (nc.sbuf_tensor(name, shape, dt, side=side) if where == "sbuf"
              else nc.psum_tensor(name, shape, dt))
        t = cm.__enter__()
        live.append((name, cm))
        return t

    def free(*names):
        for nm in names:
            for i, (n, cm) in enumerate(live):
                if n == nm:
                    cm.__exit__(None, None, None)
                    live.pop(i)
                    break

    PBIG = alloc("PBIG", [128, 2048], F32, "psum")
    pM = [PBIG[0:128, 0:512], PBIG[0:128, 512:1024]]
    pN = [PBIG[0:128, 1024:1536], PBIG[0:128, 1536:2048]]
    PO = [alloc("PO0", [128, 512], F32, "psum"), alloc("PO1", [128, 512], F32, "psum")]
    pX = [alloc(f"pX{i}", [128, 1024], BF16, "psum") for i in range(2)]

    ident = alloc("ident", [128, 128], BF16)
    sh8 = alloc("sh8", [128, 1], I32)
    gix = alloc("gix", [128, 32], I32)
    G = alloc("G", [128, 32 * 32], BF16)
    nzs = alloc("nzs", [32, D], F16)
    CS = alloc("CS", [32, T], F16)
    bqs = alloc("bqs", [128, 8], F32)
    bkt = alloc("bkt", [128, 8], F32)
    bvt = alloc("bvt", [128, 8], F32)
    XN = alloc("XN", [128, 32 * T], BF16, side="right")

    epsb = alloc("epsb", [128, 1], F32)
    iot = alloc("iot", [128, 32], F32)
    gixf = alloc("gixf", [128, 32], F32)
    qzrt = alloc("qzrt", [32, 512], I32)
    zsh = alloc("zsh", [32, D], I32)
    sclt = alloc("sclt", [32, D], F16)
    zint = alloc("zint", [32, D], I32)
    zf = alloc("zf", [32, D], F32)
    gtmp = alloc("gtmp", [128, 32], F32)
    bqraw = alloc("bqraw", [128, 32], F32)

    S.emit(sp, lambda e: e.dma_start(out=ident[:], in_=identc), inc=True)
    S.emit(sp, lambda e: e.dma_start(out=sh8[:], in_=shiftc), inc=True)
    for c in range(32):
        S.emit(sp, lambda e, c=c: e.dma_start(
            out=gix[:, c:c + 1], in_=gidx[c * 128:(c + 1) * 128].unsqueeze(1)),
            inc=True)
    S.emit(sp, lambda e: e.dma_start(out=iot[:], in_=iotac), inc=True)
    S.emit(sp, lambda e: e.dma_start(out=qzrt[:], in_=qzr), inc=True)
    S.emit(sp, lambda e: e.dma_start(
        out=zsh[:], in_=zshc.broadcast_to([32, D])), inc=True)
    S.emit(sp, lambda e: e.dma_start(out=sclt[:], in_=scl), inc=True)
    S.emit(sp, lambda e: e.dma_start(out=bqraw[:], in_=bq), inc=True)
    S.emit(sp, lambda e: e.dma_start(out=bkt[:], in_=bk), inc=True)
    S.emit(sp, lambda e: e.dma_start(out=bvt[:], in_=bv), inc=True)
    all0 = S.tok(sp)

    S.emit(dv, lambda e: e.memset(epsb[:], 1e-5), inc=True)

    gcv = S.emit(ac, lambda e: e.copy(gixf[:], gix[:]), waits=[all0], inc=True)
    g_free = None
    for c in range(32):
        ws = [all0, gcv] + ([g_free] if g_free else [])
        tk = S.emit(dv, lambda e, c=c: e.tensor_scalar(
            gtmp[:], iot[:], gixf[:, c:c + 1], None, op0=ALU.is_equal),
            waits=ws, inc=True)
        g_free = S.emit(ac, lambda e, c=c: e.copy(
            G[:, c * 32:(c + 1) * 32], gtmp[:]), waits=[tk], inc=True)
    t1 = S.emit(dv, lambda e: e.tensor_tensor(
        zint[:], qzrt[:].unsqueeze(2).broadcast_to([32, 512, 8]),
        zsh[:].rearrange("p (a b) -> p a b", b=8),
        op=ALU.logical_shift_right), waits=[all0], inc=True)
    t1 = S.emit(dv, lambda e: e.tensor_single_scalar(
        zint[:], zint[:], 15, op=ALU.bitwise_and), waits=[t1], inc=True)
    t2 = S.emit(ac, lambda e: e.copy(zf[:], zint[:]), waits=[t1], inc=True)
    t3 = S.emit(dv, lambda e: e.tensor_tensor(
        zf[:], zf[:], sclt[:], op=ALU.mult), waits=[t2], inc=True)
    S.emit(ac, lambda e: e.mul(nzs[:], zf[:], -1.0), waits=[t3], inc=True)
    t5 = S.emit(dv, lambda e: e.reduce_sum(
        bqs[:], bqraw[:].rearrange("p (a b) -> p a b", b=4), axis=AX),
        waits=[all0], inc=True)

    S.barrier()
    free("bqraw", "gtmp", "zf", "zint", "sclt", "zsh", "qzrt", "gixf", "iot")

    xtile = [alloc("xtile0", [128, D], F32), alloc("xtile1", [128, D], F32)]
    xnt = alloc("xnt", [128, D], BF16)
    sqs = alloc("sqs", [128, D], BF16)
    ssum = alloc("ssum", [128, 1], F32)
    ssq = alloc("ssq", [128, 1], F32)
    nmean = alloc("nmean", [128, 1], F32)
    varb = alloc("varb", [128, 1], F32)
    msq = alloc("msq", [128, 1], F32)
    rstd = alloc("rstd", [128, 1], F32)

    xload = []
    for i in range(2):
        xload.append(S.emit(sp, lambda e, i=i: e.dma_start(
            out=xtile[i][:], in_=x_in[i * 128:(i + 1) * 128, :]), inc=True,
            sem="st%d" % (i % 2)))
    last_evac = {}
    prev_tr = None
    for i in range(8):
        xt = xtile[i % 2]
        ldtok = xload[i]
        a = S.emit(dv, lambda e, xt=xt: e.reduce_sum(
            ssum[:], xt[:], axis=AX, negate=True), waits=[ldtok], inc=True)
        b = S.emit(ac, lambda e, xt=xt: e.activation(
            sqs[:], xt[:], AF.Square, accum_out=ssq[:]), waits=[ldtok], inc=True)
        c1 = S.emit(dv, lambda e: e.tensor_single_scalar(
            nmean[:], ssum[:], 1.0 / D, op=ALU.mult), waits=[a], inc=True)
        c2 = S.emit(dv, lambda e: e.tensor_tensor(
            msq[:], nmean[:], nmean[:], op=ALU.mult), waits=[c1], inc=True)
        c3 = S.emit(dv, lambda e: e.scalar_tensor_tensor(
            varb[:], ssq[:], 1.0 / D, msq[:], op0=ALU.mult, op1=ALU.subtract),
            waits=[b, c2], inc=True)
        c4 = S.emit(ac, lambda e: e.activation(
            varb[:], varb[:], AF.Ln, bias=epsb[:, 0:1]), waits=[c3], inc=True)
        c5 = S.emit(ac, lambda e: e.activation(
            rstd[:], varb[:], AF.Exp, scale=-0.5), waits=[c4], inc=True)
        ws6 = [c5] + ([prev_tr] if prev_tr else [])
        c6 = S.emit(dv, lambda e, xt=xt: e.tensor_scalar(
            xnt[:], xt[:], nmean[:, 0:1], rstd[:, 0:1],
            op0=ALU.add, op1=ALU.mult), waits=ws6, inc=True)
        if i + 2 < 8:
            xload.append(S.emit(sp, lambda e, i=i: e.dma_start(
                out=xtile[i % 2][:], in_=x_in[(i + 2) * 128:(i + 3) * 128, :]),
                waits=[c6], inc=True, sem="st%d" % (i % 2)))
        for c in range(32):
            bk_i = (i * 32 + c) % 2
            ws = [c6] + ([last_evac[bk_i]] if bk_i in last_evac else [])
            tt = S.emit(pe, lambda e, c=c, bk_i=bk_i: e.transpose(
                pX[bk_i][:, 0:128], xnt[:, c * 128:(c + 1) * 128], ident[:]),
                waits=ws, inc=True)
            eng = dv if (c % 2) else ac
            mk = copy_on(eng)
            ev = S.emit(eng, mk(
                XN[:, c * T + i * 128: c * T + (i + 1) * 128], pX[bk_i][:, 0:128]),
                waits=[tt], inc=True)
            last_evac[bk_i] = ev
        prev_tr = S.tok(pe)
        if DEBUG_SERIAL_P1:
            S.barrier()
    S.barrier()
    free("rstd", "msq", "varb", "nmean", "ssq", "ssum", "sqs", "xnt",
         "xtile1", "xtile0")
    if "xn" in dbg:
        S.emit(sp, lambda e: e.dma_start(out=dbg["xn"], in_=XN[:]), inc=True)
        S.barrier()
    if phases <= 1:
        return finish(nc, S, live)

    QSI = alloc("QSI", [128, 8 * T], BF16)
    KI = alloc("KI", [128, 8 * T], BF16)
    VI = alloc("VI", [128, 9 * T], BF16)
    WG = alloc("WG", [128, 2 * 32 * 256], BF16)
    WQF = alloc("WQF", [128, 8 * 1024], BF16)
    invc = alloc("invc", [128, 1], F32)
    S.emit(dv, lambda e: e.memset(invc[:], INV_SQRT_D), inc=True)

    S.emit(dv, lambda e: e.memset(VI[:], 1.0), inc=True)
    vi_ones = S.tok(dv)

    groups = ([("qs", g) for g in range(4)] + [("k", g) for g in range(4)]
              + [("v", g) for g in range(4)])
    wq_v = wq.rearrange("e (hk j) d -> e hk j d", j=4)
    group_done = {}
    wg_ready = [None] * 32
    wqf_free = [None] * 8

    for gi, (kind, g) in enumerate(groups):
        wgb = (gi % 2) * 32 * 256
        if kind != "qs":
            S.fence(gp)
        for ei in range(32):
            w_wait = []
            if gi >= 2:
                w_wait.append(group_done[gi - 2])
            if kind == "qs":
                slot = ei % 8
                ws = list(w_wait)
                if wqf_free[slot] is not None:
                    ws.append(wqf_free[slot])
                src = wq[ei * 128:(ei + 1) * 128, 8 * g:8 * g + 8, :]
                ldq = S.emit(gp, lambda e, src=src, slot=slot: e.dma_start(
                    out=WQF[:, slot * 1024:(slot + 1) * 1024], in_=src),
                    waits=ws, inc=True, sem="st%d" % slot)
                v4 = WQF[:, slot * 1024:(slot + 1) * 1024].rearrange(
                    "p (hk j d) -> p hk j d", j=4, d=128)
                wgslot = WG[:, wgb + ei * 256:wgb + (ei + 1) * 256].rearrange(
                    "p (hk d) -> p hk d", d=128)
                S.emit(dv, lambda e, v4=v4, wgslot=wgslot: e.tensor_tensor(
                    wgslot, v4[:, :, 0, :], v4[:, :, 1, :], op=ALU.add),
                    waits=[ldq] + w_wait, inc=True)
                S.emit(dv, lambda e, v4=v4, wgslot=wgslot: e.tensor_tensor(
                    wgslot, wgslot, v4[:, :, 2, :], op=ALU.add), inc=True)
                wg_ready[ei] = S.emit(dv, lambda e, v4=v4, wgslot=wgslot:
                    e.tensor_tensor(wgslot, wgslot, v4[:, :, 3, :], op=ALU.add),
                    inc=True)
                wqf_free[slot] = wg_ready[ei]
            else:
                wsrc = wk if kind == "k" else wv
                src = wsrc[ei * 128:(ei + 1) * 128, 2 * g:2 * g + 2, :]
                wg_ready[ei] = S.emit(gp, lambda e, src=src, ei=ei, wgb=wgb:
                    e.dma_start(out=WG[:, wgb + ei * 256:wgb + (ei + 1) * 256],
                                in_=src), waits=w_wait, inc=True)
        grp_load_tok = wg_ready[31]
        banks = [pM[0], pM[1], pN[0], pN[1]]
        for ei in range(32):
            for mi in range(2):
                for hf in range(2):
                    bkk = banks[mi * 2 + hf]
                    ws = []
                    if mi == 0 and hf == 0:
                        ws.append(wg_ready[ei] if kind == "qs" else grp_load_tok)
                        if ei == 0 and gi >= 1:
                            ws.append(group_done[gi - 1])
                    S.emit(pe, lambda e, ei=ei, mi=mi, hf=hf, bkk=bkk, wgb=wgb:
                        e.matmul(
                        bkk,
                        WG[:, wgb + ei * 256 + mi * 128:
                           wgb + ei * 256 + (mi + 1) * 128],
                        XN[:, ei * T + hf * 512: ei * T + hf * 512 + 512],
                        start=(ei == 0), stop=(ei == 31)),
                        waits=ws, inc=(ei == 31 and mi == 1 and hf == 1))
        mm_tok = S.tok(pe)
        for mi in range(2):
            for hf in range(2):
                bkk = banks[mi * 2 + hf]
                hk = 2 * g + mi
                if kind == "qs":
                    dst = QSI[:].rearrange("p (t j) -> p t j", j=8)[
                        :, hf * 512:(hf + 1) * 512, hk]
                    bias_col = bqs[:, hk:hk + 1]
                elif kind == "k":
                    dst = KI[:].rearrange("p (t j) -> p t j", j=8)[
                        :, hf * 512:(hf + 1) * 512, hk]
                    bias_col = bkt[:, hk:hk + 1]
                else:
                    dst = VI[:].rearrange("p (t j) -> p t j", j=9)[
                        :, hf * 512:(hf + 1) * 512, hk]
                    bias_col = bvt[:, hk:hk + 1]
                w = [mm_tok] + ([vi_ones] if kind == "v" else [])
                if kind == "qs":
                    S.emit(dv, lambda e, dst=dst, bkk=bkk, bias_col=bias_col:
                           e.tensor_scalar(dst, bkk, bias_col, invc[:, 0:1],
                                           op0=ALU.add, op1=ALU.mult),
                           waits=w, inc=True)
                else:
                    S.emit(dv, lambda e, dst=dst, bkk=bkk, bias_col=bias_col:
                           e.tensor_scalar(dst, bkk, bias_col, None, op0=ALU.add),
                           waits=w, inc=True)
        group_done[gi] = S.tok(dv)
        if SERIAL_P2:
            S.barrier()
    S.barrier()
    free("invc", "WQF", "WG")
    for nm, buf in (("qs", QSI), ("kk", KI), ("vv", VI)):
        if nm in dbg:
            S.emit(sp, lambda e, nm=nm, buf=buf: e.dma_start(
                out=dbg[nm], in_=buf[:]), inc=True)
    if any(n in dbg for n in ("qs", "kk", "vv")):
        S.barrier()
    if phases <= 2:
        return finish(nc, S, live)

    srf = [alloc(f"srf{i}", [128, D], BF16) for i in range(2)]

    sg_tok = None
    for k in range(32):
        pp = k % 2
        ws = []
        if sg_tok is not None and k >= 2:
            ws.append(sgd[k - 2])
        g1 = S.emit(gp, lambda e, k=k, pp=pp: e.indirect_dma_start(
            out=srf[pp][:], out_offset=None, in_=scl,
            in_offset=IndirectOffsetOnAxis(gix[:, k:k + 1], 0)),
            waits=ws, inc=True, sem="st%d" % (8 + pp))
        if k == 0:
            sgd = {}
        sgd[k] = S.emit(sp, lambda e, k=k, pp=pp: e.dma_start(
            out=srd[k], in_=srf[pp][:]), waits=[g1], inc=True,
            sem="st%d" % (10 + pp))
        sg_tok = sgd[k]
    sg_all = [sgd[30], sgd[31]]

    QSw = [alloc(f"QSw{i}", [128, 128], BF16) for i in range(2)]
    Kw = [alloc(f"Kw{i}", [128, 128], BF16) for i in range(2)]
    QTa = [alloc(f"QTa{i}", [128, 128], BF16) for i in range(2)]
    QTb = [alloc(f"QTb{i}", [128, 128], BF16) for i in range(2)]
    KT = [alloc(f"KT{i}", [128, 128], BF16) for i in range(2)]
    A4 = [alloc(f"A4{i}", [128, 512], BF16) for i in range(2)]
    m4 = [alloc(f"m4{i}", [128, 4], F32) for i in range(2)]
    OT = alloc("OT", [9, 128 * 128], BF16)

    for i in range(2):
        S.emit(dv, lambda e, i=i: e.memset(Kw[i][:], 1.0), inc=True)
    kinit = S.tok(dv)

    import os as _os
    NW = int(_os.environ.get("NW_LIMIT", str(T // 4)))
    tok_build = [None] * NW
    tok_mcopy = [None] * NW
    tok_evA = [None] * NW
    tok_evK = [None] * NW
    tok_evB = [None] * NW
    tok_S = [None] * NW
    tok_ST = [None] * NW
    tok_max = [None] * NW
    tok_exp = [None] * NW
    tok_out = [None] * NW
    tok_oev = [None] * NW
    tok_dump = {}
    trb = [0]

    def emit_build(w):
        b = w % 2
        ws = [kinit]
        if w >= 2:
            ws += [tok_evB[w - 2], tok_evK[w - 2]]
        t0 = w * 4
        s_q = QSI[:, 8 * t0: 8 * t0 + 32].rearrange("p (a c) -> p a c", c=8)
        d_q = QSw[b][:].rearrange("p (a c) -> p a c", c=32)[:, :, 0:8]
        S.emit(dv, lambda e, s_q=s_q, d_q=d_q: e.tensor_copy(d_q, s_q),
               waits=ws, inc=True)
        s_k = KI[:, 8 * t0: 8 * t0 + 32].rearrange("p (a c) -> p a c", c=8)
        d_k = Kw[b][:].rearrange("p (a c) -> p a c", c=32)[:, :, 0:8]
        tok_build[w] = S.emit(dv, lambda e, s_k=s_k, d_k=d_k: e.tensor_copy(
            d_k, s_k), inc=True)

    def tr_into(src_tile, dst_tile, wait_toks):
        i0 = trb[0] % 2
        trb[0] += 1
        ws = list(wait_toks)
        if i0 in last_evac:
            ws.append(last_evac[i0])
        tt = S.emit(pe, lambda e, i0=i0, src_tile=src_tile: e.transpose(
            pX[i0][:, 0:128], src_tile, ident[:]), waits=ws, inc=True)
        ev = S.emit(ac, lambda e, i0=i0, dst_tile=dst_tile: e.copy(
            dst_tile, pX[i0][:, 0:128]), waits=[tt], inc=True)
        last_evac[i0] = ev
        return ev

    def emit_tA_tK(w):
        b = w % 2
        tok_evA[w] = tr_into(QSw[b][:], QTa[b][:], [tok_build[w]])
        tok_evK[w] = tr_into(Kw[b][:], KT[b][:], [tok_build[w]])

    def emit_S(w):
        b = w % 2
        co = 256 * (w % 2)
        ws = [tok_evA[w], tok_evK[w]]
        if w >= 1:
            ws += [tok_exp[w - 1], tok_max[w - 1]]
        for tl in range(4):
            S.emit(pe, lambda e, b=b, tl=tl, co=co: e.matmul(
                PBIG[0:128, 512 * tl + co: 512 * tl + co + 128],
                QTa[b][32 * tl:32 * tl + 8, :],
                KT[b][32 * tl:32 * tl + 8, :],
                start=True, stop=True, tile_position=(32 * tl, 0)),
                waits=ws if tl == 0 else [], inc=(tl == 3))
        tok_S[w] = S.tok(pe)

    def emit_max(w):
        b = w % 2
        co = 256 * (w % 2)
        sview = PBIG[:].rearrange("p (a c) -> p a c", a=4)[:, :, co:co + 128]
        tok_max[w] = S.emit(dv, lambda e, b=b, sview=sview: e.reduce_max(
            m4[b][:], sview, axis=AX,
            negate=True), waits=[tok_S[w]], inc=True)
        d_m = QSw[b][:].rearrange("p (a c) -> p a c", c=32)[:, :, 8]
        tok_mcopy[w] = S.emit(dv, lambda e, b=b, d_m=d_m: e.tensor_copy(
            d_m.unsqueeze(2), m4[b][:].unsqueeze(2)), waits=[tok_max[w]],
            inc=True)

    def emit_tB(w):
        b = w % 2
        tok_evB[w] = tr_into(QSw[b][:], QTb[b][:], [tok_mcopy[w]])

    def emit_ST(w):
        b = w % 2
        co = 256 * (w % 2) + 128
        ws = [tok_evB[w]]
        for tl in range(4):
            S.emit(pe, lambda e, b=b, tl=tl, co=co: e.matmul(
                PBIG[0:128, 512 * tl + co: 512 * tl + co + 128],
                KT[b][32 * tl:32 * tl + 9, :],
                QTb[b][32 * tl:32 * tl + 9, :],
                start=True, stop=True, tile_position=(32 * tl, 0)),
                waits=ws if tl == 0 else [], inc=(tl == 3))
        tok_ST[w] = S.tok(pe)

    def emit_exp(w):
        b = w % 2
        co = 256 * (w % 2) + 128
        stview = PBIG[:].rearrange("p (a c) -> p a c", a=4)[:, :, co:co + 128]
        dstv = A4[b][:].rearrange("p (a c) -> p a c", a=4)
        ws = [tok_ST[w]]
        if w >= 1:
            ws.append(tok_out[w - 1])
        tok_exp[w] = S.emit(ac, lambda e, b=b, stview=stview, dstv=dstv:
            e.activation(dstv, stview, AF.Exp), waits=ws, inc=True)

    def emit_out(w):
        b = w % 2
        po = PO[w % 2]
        ws = [tok_exp[w], tok_max[w]]
        if w >= 2:
            ws.append(tok_oev[w - 2])
        for tl in range(4):
            t = w * 4 + tl
            S.emit(pe, lambda e, po=po, b=b, tl=tl, t=t: e.matmul(
                po[0:9, tl * 128:(tl + 1) * 128],
                VI[:, 9 * t: 9 * t + 9],
                A4[b][:, tl * 128:(tl + 1) * 128],
                start=True, stop=True),
                waits=ws if tl == 0 else [], inc=(tl == 3))
        tok_out[w] = S.tok(pe)

    def emit_oev(w):
        po = PO[w % 2]
        wl = w % 32
        seg = w // 32
        ws = [tok_out[w]]
        if (seg - 1) in tok_dump:
            ws.append(tok_dump[seg - 1])
        src = po[0:9, :].rearrange("p (t d) -> p d t", t=4)
        dst = OT[:].rearrange("p (d t) -> p d t", t=128)[:, :, 4 * wl:4 * wl + 4]
        tok_oev[w] = S.emit(dv, lambda e, src=src, dst=dst: e.tensor_copy(
            dst, src), waits=ws, inc=True)

    def emit_dump(seg):
        ws = [tok_oev[seg * 32 + 31]]
        src8 = OT[0:8, :].rearrange("p (d t) -> p d t", t=128)
        S.emit(sp, lambda e, src8=src8, seg=seg: e.dma_start(
            out=od[:, :, seg * 128:(seg + 1) * 128], in_=src8),
            waits=ws, inc=True)
        srcz = OT[8:9, :].rearrange("p (d t) -> p d t", t=128)
        tok_dump[seg] = S.emit(sp, lambda e, srcz=srcz, seg=seg: e.dma_start(
            out=oz[:, seg * 128:(seg + 1) * 128].unsqueeze(0), in_=srcz), inc=True)

    emit_build(0)
    emit_tA_tK(0)
    emit_S(0)
    emit_build(1)
    emit_tA_tK(1)
    emit_max(0)
    emit_tB(0)
    emit_ST(0)
    emit_exp(0)
    for w in range(NW):
        if w + 1 < NW:
            emit_S(w + 1)
            emit_max(w + 1)
        if w + 2 < NW:
            emit_build(w + 2)
            emit_tA_tK(w + 2)
        emit_out(w)
        emit_oev(w)
        if w + 1 < NW:
            emit_tB(w + 1)
            emit_ST(w + 1)
            emit_exp(w + 1)
        if w % 32 == 31:
            emit_dump(w // 32)
        if SERIAL_P3:
            S.barrier()
    S.barrier()
    free("OT", "m41", "m40", "A41", "A40", "KT1", "KT0", "QTb1", "QTb0",
         "QTa1", "QTa0", "Kw1", "Kw0", "QSw1", "QSw0", "srf1", "srf0",
         "VI", "KI", "QSI")

    if phases <= 3:
        return finish(nc, S, live)

    NPAR = 8
    ACC = alloc("ACC", [128, 32 * T], BF16)
    qwr = [alloc(f"qwr{i}", [128, 256], I32) for i in range(NPAR)]
    srep = [alloc(f"srep{i}", [128, 256], BF16) for i in range(NPAR)]
    nibi = [alloc(f"nibi{i}", [128, 256], I32) for i in range(NPAR)]
    nibb = [alloc(f"nibb{i}", [128, 256], BF16) for i in range(NPAR)]
    Wt = [alloc(f"Wt{i}", [128, 256], BF16) for i in range(NPAR)]
    zfb = alloc("zfb", [128, T], F32)
    zfr = alloc("zfr", [128, T], F32)
    attf = [alloc(f"attf{i}", [128, T], BF16) for i in range(2)]
    zrep = [alloc(f"zrep{i}", [128, T], F32) for i in range(2)]
    tmpc = [alloc(f"tmpc{i}", [128, 512], F32) for i in range(2)]

    zl = S.emit(gp, lambda e: e.dma_start(out=zfb[:], in_=oz), inc=True,
                sem="st8")
    zr_ = S.emit(dv, lambda e: e.reciprocal(zfr[:], zfb[:]), waits=[zl],
                 inc=True)
    zd = S.emit(sp, lambda e: e.dma_start(out=zrd, in_=zfr[:]), waits=[zr_],
                inc=True, sem="st10")

    cs_half = []
    for hf in range(2):
        ws0 = [cs_half[-1]] if cs_half else []
        for k in range(32):
            S.emit(pe, lambda e, hf=hf, k=k: e.matmul(
                PBIG[0:32, 0:512], G[:, k * 32:(k + 1) * 32],
                XN[:, k * T + hf * 512: k * T + hf * 512 + 512],
                start=(k == 0), stop=(k == 31)),
                waits=ws0 if k == 0 else [], inc=(k == 31))
        cst = S.tok(pe)
        cs_half.append(S.emit(dv, lambda e, hf=hf: e.tensor_copy(
            CS[:, hf * 512:(hf + 1) * 512], PBIG[0:32, 0:512]),
            waits=[cst], inc=True))
    cs_done = cs_half[-1]

    banksF = [pM[0], pM[1], pN[0], pN[1]]
    deq_free = [None] * NPAR
    comb_done = {}
    rb_att = [None, None]
    rb_z = [None, None]
    for jg in range(16):
        n0 = jg * 256
        for jl in range(2):
            F_ = 2 * jg + jl
            ws = [zd]
            if (jl, 0) in comb_done:
                ws += [comb_done[(jl, 0)], comb_done[(jl, 1)]]
            last = None
            for dl in range(4):
                asrc = od[:, 4 * F_ + dl, :].unsqueeze(1).broadcast_to([8, 4, T])
                last = S.emit(gp, lambda e, asrc=asrc, dl=dl, jl=jl:
                              e.dma_start(out=attf[jl][32 * dl:32 * dl + 32, :],
                                          in_=asrc),
                              waits=ws if dl == 0 else [], inc=True,
                              sem="st%d" % (8 + jl))
            rb_att[jl] = last
            zsrc = zrd[4 * F_:4 * F_ + 4, :].unsqueeze(1).broadcast_to(
                [4, 32, T])
            rb_z[jl] = S.emit(gp, lambda e, zsrc=zsrc, jl=jl: e.dma_start(
                out=zrep[jl][:], in_=zsrc), waits=ws, inc=True,
                sem="st%d" % (10 + jl))
        for k in range(32):
            par = k % NPAR
            ws = [deq_free[par]] if deq_free[par] is not None else []
            if jg == 0 and k < 2:
                ws += sg_all
            S.emit(sp, lambda e, k=k, n0=n0, par=par: e.dma_start(
                out=qwr[par][:],
                in_=qwt[16 * k:16 * k + 16, n0:n0 + 256].unsqueeze(1)
                .broadcast_to([16, 8, 256])), waits=ws, inc=True,
                sem="st%d" % par)
            ld2 = S.emit(sp, lambda e, k=k, n0=n0, par=par: e.dma_start(
                out=srep[par][:], in_=srd[k, :, n0:n0 + 256]),
                inc=True, sem="st%d" % par)
            d1 = S.emit(dv, lambda e, par=par: e.tensor_scalar(
                nibi[par][:], qwr[par][:], sh8[:, 0:1], 15,
                op0=ALU.logical_shift_right, op1=ALU.bitwise_and),
                waits=[ld2] + ws, inc=True)
            d2 = S.emit(ac, lambda e, par=par: e.copy(
                nibb[par][:], nibi[par][:]), waits=[d1], inc=True)
            d3 = S.emit(dv, lambda e, par=par: e.tensor_tensor(
                Wt[par][:], nibb[par][:], srep[par][:], op=ALU.mult),
                waits=[d2], inc=True)
            for jl in range(2):
                for hf in range(2):
                    bkk = banksF[jl * 2 + hf]
                    ws2 = []
                    if jl == 0 and hf == 0:
                        ws2.append(d3)
                        if k == 0 and jg == 0:
                            ws2.append(cs_done)
                    if k == 0 and (jl, hf) in comb_done:
                        ws2.append(comb_done[(jl, hf)])
                    S.emit(pe, lambda e, par=par, jl=jl, hf=hf, bkk=bkk, k=k:
                           e.matmul(bkk,
                                    Wt[par][:, jl * 128:(jl + 1) * 128],
                                    XN[:, k * T + hf * 512: k * T + hf * 512 + 512],
                                    start=(k == 0), stop=False),
                           waits=ws2,
                           inc=(jl == 1 and hf == 1))
            deq_free[par] = S.tok(pe)
        for jl in range(2):
            for hf in range(2):
                bkk = banksF[jl * 2 + hf]
                j = 2 * jg + jl
                S.emit(pe, lambda e, j=j, hf=hf, bkk=bkk: e.matmul(
                    bkk, nzs[:, j * 128:(j + 1) * 128],
                    CS[:, hf * 512:(hf + 1) * 512],
                    start=False, stop=True),
                    inc=(jl == 1 and hf == 1))
        fmt = S.tok(pe)
        for jl in range(2):
            F_ = 2 * jg + jl
            for hf in range(2):
                bkk = banksF[jl * 2 + hf]
                hs = slice(hf * 512, hf * 512 + 512)
                m1 = S.emit(dv, lambda e, jl=jl, hf=hf, hs=hs: e.tensor_tensor(
                    tmpc[jl][:], attf[jl][:, hs], zrep[jl][:, hs],
                    op=ALU.mult),
                    waits=[fmt, rb_att[jl], rb_z[jl]], inc=True)
                comb_done[(jl, hf)] = S.emit(
                    dv, lambda e, jl=jl, hf=hf, F_=F_, bkk=bkk: e.tensor_tensor(
                        ACC[:, F_ * T + hf * 512: F_ * T + hf * 512 + 512],
                        tmpc[jl][:], bkk, op=ALU.add), waits=[m1], inc=True)
    S.barrier()
    for i in reversed(range(2)):
        free(f"tmpc{i}")
    for i in reversed(range(2)):
        free(f"zrep{i}")
    for i in reversed(range(2)):
        free(f"attf{i}")
    free("zfr", "zfb")
    for i in reversed(range(NPAR)):
        free(f"Wt{i}")
    for i in reversed(range(NPAR)):
        free(f"nibb{i}")
    for i in reversed(range(NPAR)):
        free(f"nibi{i}")
    for i in reversed(range(NPAR)):
        free(f"srep{i}")
    for i in reversed(range(NPAR)):
        free(f"qwr{i}")
    free("XN")
    if "cs" in dbg:
        S.emit(sp, lambda e: e.dma_start(out=dbg["cs"], in_=CS[:]), inc=True)
    if "acc" in dbg:
        S.emit(sp, lambda e: e.dma_start(out=dbg["acc"], in_=ACC[:]), inc=True)
    if any(n in dbg for n in ("cs", "acc")):
        S.barrier()
    if phases <= 4:
        return finish(nc, S, live)

    xt2 = [alloc("xt2_0", [128, D], F32), alloc("xt2_1", [128, D], F32)]
    xpb = alloc("xpb", [128, D], F32)
    otok = alloc("otok", [128, D], F32)
    brep = alloc("brep", [128, D], F16)

    bl = S.emit(sp, lambda e: e.dma_start(
        out=brep[:], in_=fbias.unsqueeze(0).broadcast_to([128, D])), inc=True,
        sem="st2")
    x2load = []
    for i in range(2):
        x2load.append(S.emit(sp, lambda e, i=i: e.dma_start(
            out=xt2[i][:], in_=x_in[i * 128:(i + 1) * 128, :]), inc=True,
            sem="st%d" % (i % 2)))
    le5 = {}
    st_tok = None
    prev_evs = None
    for i in range(8):
        xb_ = xt2[i % 2]
        wsx = [x2load[i], bl] + ([prev_evs] if prev_evs else [])
        xp = S.emit(dv, lambda e, xb_=xb_: e.tensor_tensor(
            xpb[:], xb_[:], brep[:], op=ALU.add), waits=wsx, inc=True)
        if i + 2 < 8:
            x2load.append(S.emit(sp, lambda e, i=i: e.dma_start(
                out=xt2[i % 2][:], in_=x_in[(i + 2) * 128:(i + 3) * 128, :]),
                waits=[xp], inc=True, sem="st%d" % (i % 2)))
        evs = []
        for F_ in range(32):
            bk_i = (i * 32 + F_) % 2
            ws = [xp]
            if st_tok is not None:
                ws.append(st_tok)
            if bk_i in le5:
                ws.append(le5[bk_i])
            tt = S.emit(pe, lambda e, F_=F_, i=i, bk_i=bk_i: e.transpose(
                pX[bk_i][:, 0:128],
                ACC[:, F_ * T + i * 128: F_ * T + (i + 1) * 128], ident[:]),
                waits=ws, inc=True)
            ev = S.emit(dv, lambda e, F_=F_, bk_i=bk_i: e.tensor_tensor(
                otok[:, F_ * 128:(F_ + 1) * 128], pX[bk_i][:, 0:128],
                xpb[:, F_ * 128:(F_ + 1) * 128], op=ALU.add),
                waits=[tt, xp], inc=True)
            le5[bk_i] = ev
            evs.append(ev)
        st_tok = S.emit(sp, lambda e, i=i: e.dma_start(
            out=out[i * 128:(i + 1) * 128, :], in_=otok[:]),
            waits=evs, inc=True, sem="st3")
        if SERIAL_P5B:
            S.barrier()
        ac_last = max((v for (s, v) in evs if s == "ac"), default=0)
        prev_evs = ("ac", ac_last)
    S.emit(sp, lambda e: e.nop(), waits=[st_tok])
    S.barrier()
    return finish(nc, S, live)


def finish(nc, S, live):
    from contextlib import ExitStack
    with ExitStack() as ctx:
        for e in ALLSEMS:
            S.sems[e] = ctx.enter_context(nc.semaphore("sem_" + e))
        with nc.Block() as blk0:
            @blk0.gpsimd
            def _(g):
                for snm in ALLSEMS:
                    g.sem_clear(S.sems[snm])
        block = ctx.enter_context(nc.Block())
        S.replay(block)
    for n, cm in reversed(live):
        cm.__exit__(None, None, None)
    return nc


def make_inputs(inp, core):
    bf = ml_dtypes.bfloat16
    x = np.ascontiguousarray(np.asarray(inp["L0_attn_x"]).reshape(8192, 4096)[
        core * T:(core + 1) * T])
    return dict(
        x=x,
        wq=np.ascontiguousarray(np.transpose(np.asarray(inp["L0_attn_Wq"]), (0, 2, 1))),
        wk=np.ascontiguousarray(np.transpose(np.asarray(inp["L0_attn_Wk"]), (0, 2, 1))),
        wv=np.ascontiguousarray(np.transpose(np.asarray(inp["L0_attn_Wv"]), (0, 2, 1))),
        bq=np.ascontiguousarray(np.asarray(inp["L0_attn_bq"])),
        bk=np.ascontiguousarray(np.asarray(inp["L0_attn_bk"])),
        bv=np.ascontiguousarray(np.asarray(inp["L0_attn_bv"])),
        qwt=np.ascontiguousarray(np.asarray(inp["L0_ffn_qweight"])),
        qzr=np.ascontiguousarray(np.asarray(inp["L0_ffn_qzeros"])),
        scl=np.ascontiguousarray(np.asarray(inp["L0_ffn_scales"])),
        gidx=np.ascontiguousarray(np.asarray(inp["L0_ffn_g_idx"])),
        fbias=np.ascontiguousarray(np.asarray(inp["L0_ffn_bias"])),
        identc=np.eye(128, dtype=bf),
        shiftc=(4 * (np.arange(128, dtype=np.int32) % 8)).reshape(128, 1),
        iotac=np.broadcast_to(np.arange(32, dtype=np.float32), (128, 32)).copy(),
        zshc=(4 * (np.arange(4096, dtype=np.int32) % 8)).reshape(1, 4096),
    )


def kernel(**inputs):
    nc = build_nc()
    in_maps = [make_inputs(inputs, c) for c in range(NC)]
    res = run_bass_kernel_spmd(nc, in_maps, core_ids=list(range(NC)))
    outs = [np.asarray(res.results[c]["out"]) for c in range(NC)]
    return np.concatenate(outs, axis=0).reshape(4, 2048, 4096).astype(np.float32)
